# version 01
# baseline (speedup 1.0000x reference)
import numpy as np
import ml_dtypes
from contextlib import ExitStack

import concourse.bass as bass
import concourse.mybir as mybir
from concourse.bass_utils import run_bass_kernel_spmd

F32 = mybir.dt.float32
BF16 = mybir.dt.bfloat16
AF = mybir.ActivationFunctionType
ALU = mybir.AluOpType
AX = mybir.AxisListType
BFNP = ml_dtypes.bfloat16

D = 1024
NK = 8
H = 4
DH = 256
NE = 8
DF = 4096
NFQ = 4
CC = 128
JIT2 = 0.02
NEG = -1e30

C_EPS, C_L2EPS, C_INV256, C_ZERO, C_NEG1, C_INVD = 0, 1, 2, 3, 4, 5


def full_cfg():
    return dict(T_own=4096, ST=512, n_halo=2, n_own=2, dbg=())


class Sem:
    def __init__(self, nc, es, name):
        self.h = es.enter_context(nc.semaphore(name))
        self.n = 0

    def inc(self, instr, k=1):
        instr.then_inc(self.h, k)
        self.n += k
        return instr

    def wait(self, eng, n=None):
        eng.wait_ge(self.h, self.n if n is None else n)


def build_program(cfg):
    T_OWN = cfg["T_own"]
    ST = cfg["ST"]
    N_HALO = cfg["n_halo"]
    N_OWNH = cfg["n_own"]
    HALF = T_OWN // N_OWNH
    NCH = HALF // CC
    NST = HALF // ST
    CPS = ST // CC
    NT128 = HALF // 128
    DBG = list(cfg.get("dbg", ()))

    nc = bass.Bass("TRN2", target_bir_lowering=False, debug=False,
                   detect_race_conditions=False)

    T_TOT = (N_HALO + N_OWNH) * HALF

    hx = nc.dram_tensor("x_att", [T_TOT, D], F32, kind="ExternalInput").ap()
    hwq = nc.dram_tensor("wq", [D, D], F32, kind="ExternalInput").ap()
    hwk = nc.dram_tensor("wk", [D, D], F32, kind="ExternalInput").ap()
    hwv = nc.dram_tensor("wv", [D, D], F32, kind="ExternalInput").ap()
    hwo = nc.dram_tensor("wo", [D, D], F32, kind="ExternalInput").ap()
    hwb = nc.dram_tensor("wb", [D, H], F32, kind="ExternalInput").ap()
    hconv = nc.dram_tensor("convw", [128, NK, 3, 4], F32, kind="ExternalInput").ap()
    hgr = nc.dram_tensor("g_rms_r", [128, NK], F32, kind="ExternalInput").ap()
    hlng = nc.dram_tensor("ln_g", [1, D], F32, kind="ExternalInput").ap()
    hlnb = nc.dram_tensor("ln_b", [1, D], F32, kind="ExternalInput").ap()
    hgw = nc.dram_tensor("gate_wt", [D, NE], F32, kind="ExternalInput").ap()
    hw1 = nc.dram_tensor("w1t", [NE, D, DF], F32, kind="ExternalInput").ap()
    hw3 = nc.dram_tensor("w3t", [NE, D, DF], F32, kind="ExternalInput").ap()
    hw2 = nc.dram_tensor("w2t", [NE, DF, D], F32, kind="ExternalInput").ap()
    hcst = nc.dram_tensor("cst", [128, 8], F32, kind="ExternalInput").ap()
    hid = nc.dram_tensor("ident", [128, 128], BF16, kind="ExternalInput").ap()
    hmsu = nc.dram_tensor("mask_su", [128, 128], F32, kind="ExternalInput").ap()
    hmui = nc.dram_tensor("mask_ui", [128, 128], F32, kind="ExternalInput").ap()
    hones = nc.dram_tensor("onesrow", [1, 128], BF16, kind="ExternalInput").ap()
    hhm = nc.dram_tensor("headmask", [128, NK, H], BF16, kind="ExternalInput").ap()
    hhsel = nc.dram_tensor("headsel", [128, H, 128], BF16, kind="ExternalInput").ap()
    hout = nc.dram_tensor("out", [T_OWN, D], F32, kind="ExternalOutput").ap()
    dbg_outs = {}
    for name, shape, dt in DBG:
        dbg_outs[name] = nc.dram_tensor("dbg_" + name, list(shape), dt,
                                        kind="ExternalOutput").ap()

    es = ExitStack()

    def sbuf(name, shape, dt=BF16):
        return es.enter_context(nc.sbuf_tensor("sb_" + name, list(shape), dt))

    cst = sbuf("cst", [128, 8], F32)
    ident = sbuf("ident", [128, 128], BF16)
    msu = sbuf("msu", [128, 128], F32)
    mui = sbuf("mui", [128, 128], F32)
    onesr = sbuf("onesr", [1, 128], BF16)
    hmm = sbuf("hmm", [128, NK, H], BF16)
    hsel = sbuf("hsel", [128, H, 128], BF16)
    convw = sbuf("convw", [128, NK, 3, 4], F32)
    grms = sbuf("grms", [128, NK], F32)
    lngb = sbuf("lngb", [128, 2, D], BF16)
    lnrow = sbuf("lnrow", [1, 2, D], BF16)
    gwt = sbuf("gwt", [128, NK, NE], BF16)
    wb_sb = sbuf("wb_sb", [128, NK, H])
    xn_fm = sbuf("xn_fm", [128, NK, HALF])
    xn_tile = sbuf("xn_tile", [128, D])
    g_tm = sbuf("g_tm", [128, NT128, D])
    wtok = sbuf("wtok", [128, NT128, NE], F32)
    tails = sbuf("tails", [128, 3, NK, 3])
    beta = sbuf("beta", [128, NCH, H], F32)
    nbeta = sbuf("nbeta", [128, NCH, H], F32)
    S32 = sbuf("S32", [128, H, 2, DH], F32)
    Sbf = sbuf("Sbf", [128, H, 2, DH])
    ysb = sbuf("ysb", [128, D], F32)
    ssb = sbuf("ssb", [128, D])
    stat = sbuf("stat", [128, 16], F32)
    gsil = sbuf("gsil", [128, D])
    sxg = sbuf("sxg", [128, D])
    xres = sbuf("xres", [128, 2, D], F32)
    score = sbuf("score", [128, NE], F32)
    gt1 = sbuf("gt1", [128, NE], F32)
    gt2 = sbuf("gt2", [128, NE], F32)
    gt3 = sbuf("gt3", [128, NE], F32)
    gt4 = sbuf("gt4", [128, NE], F32)

    class Bufs:
        pass
    bufs = Bufs()

    psF = es.enter_context(nc.psum_tensor("psF", [128, 6, 512], F32))
    psT = es.enter_context(nc.psum_tensor("psT", [128, 2, 1024], BF16))

    with es, nc.Block() as block:
        s_pe = Sem(nc, es, "s_pe")
        s_ve = Sem(nc, es, "s_ve")
        s_se = Sem(nc, es, "s_se")
        s_ldx = Sem(nc, es, "s_ldx")
        s_ldw = Sem(nc, es, "s_ldw")
        s_out = Sem(nc, es, "s_out")
        s_bar = Sem(nc, es, "s_bar")
        bank_prod = [Sem(nc, es, f"bp{b}") for b in range(8)]
        bank_last = [None] * 8
        xf_marks = []

        def vfence(vector, ins):
            s_ve.inc(ins)
            vector.wait_ge(s_ve.h, s_ve.n)

        def sfence(scalar, ins):
            s_se.inc(ins)
            scalar.wait_ge(s_se.h, s_se.n)

        def pe_acquire(pe, bank):
            rec = bank_last[bank]
            if rec is not None:
                pe.wait_ge(rec[0].h, rec[1])

        def pe_produced(instr, bank):
            bank_prod[bank].inc(instr)

        def ev_acquire(eng, bank):
            eng.wait_ge(bank_prod[bank].h, bank_prod[bank].n)

        def ev_freed(bank, sem):
            bank_last[bank] = (sem, sem.n)

        def pf(bank, w=512, off=0):
            return psF[:, bank, off:off + w]

        def pt(bank, w=1024, off=0):
            return psT[:, bank - 6, off:off + w]

        def barrier():
            target = s_bar.n + 5
            for section in (block.tensor, block.vector, block.scalar,
                            block.gpsimd, block.sync):
                def go(eng):
                    eng.drain()
                    eng.sem_inc(s_bar.h, 1)
                    eng.wait_ge(s_bar.h, target)
                section(go)
            s_bar.n = target

        def dbg(name, ap):
            if name not in dbg_outs:
                return
            barrier()

            @block.sync
            def _(sync):
                s_out.inc(sync.dma_start(out=dbg_outs[name], in_=ap), 16)

        @block.sync
        def _(sync):
            for dst, hsrc in ((cst, hcst), (msu, hmsu), (mui, hmui),
                              (convw, hconv), (grms, hgr)):
                s_ldx.inc(sync.dma_start(out=dst[:], in_=hsrc[:]), 16)

        @block.gpsimd
        def _(gpsimd):
            s_ldw.inc(gpsimd.dma_start(out=ident[:], in_=hid[:]), 16)
            s_ldw.inc(gpsimd.dma_start(out=onesr[:], in_=hones[:]), 16)
            s_ldw.inc(gpsimd.dma_start(out=hmm[:], in_=hhm[:]), 16)
            s_ldw.inc(gpsimd.dma_start(out=hsel[:], in_=hhsel[:]), 16)
            s_ldw.inc(gpsimd.dma_start(out=lnrow[:, 0, :], in_=hlng[:]), 16)
            s_ldw.inc(gpsimd.dma_start(out=lnrow[:, 1, :], in_=hlnb[:]), 16)
            s_ldw.inc(gpsimd.dma_start(
                out=gwt[:], in_=hgw.rearrange("(kc p) n -> p kc n", p=128)), 16)
            s_ldw.inc(gpsimd.dma_start(
                out=wb_sb[:], in_=hwb.rearrange("(kc p) n -> p kc n", p=128)), 16)

        @block.vector
        def _(vector):
            s_ldx.wait(vector)
            s_ldw.wait(vector)
            vector.memset(S32[:], 0.0)
            vector.memset(Sbf[:], 0.0)
            vector.memset(tails[:], 0.0)
            ins = vector.memset(stat[:], 0.0)
            s_ve.inc(ins)

        @block.tensor
        def _(tensor):
            s_ve.wait(tensor)
            for j in range(2):
                for hh in range(2):
                    bank = 4 + hh
                    pe_acquire(tensor, bank)
                    ins = tensor.matmul(pf(bank), onesr[:],
                                        lnrow[:, j, hh * 512:(hh + 1) * 512],
                                        start=True, stop=True)
                    pe_produced(ins, bank)

                @block.vector
                def _(vector, j=j):
                    ev_acquire(vector, 4)
                    ev_acquire(vector, 5)
                    s_ve.inc(vector.tensor_copy(lngb[:, j, 0:512], pf(4)))
                    ev_freed(4, s_ve)
                    s_ve.inc(vector.tensor_copy(lngb[:, j, 512:1024], pf(5)))
                    ev_freed(5, s_ve)
        barrier()

        arena_ct = [0]

        def attn_arena():
            arena_ct[0] += 1
            sfx = "_%d" % arena_ct[0]
            es2 = ExitStack()

            def sb2(name, shape, dt=BF16):
                return es2.enter_context(
                    nc.sbuf_tensor("ar_" + name + sfx, list(shape), dt))
            bufs.wproj = sb2("wproj", [128, 2, NK, D])
            bufs.wo_sb = sb2("wo", [128, NK, D])
            bufs.linb = sb2("linb", [128, 2, ST + 3])
            bufs.qcv = sb2("qcv", [128, NK, ST])
            bufs.kcv = sb2("kcv", [128, NK, ST])
            bufs.vcv = sb2("vcv", [128, NK, ST])
            bufs.convacc = sb2("convacc", [128, 2, ST], F32)
            bufs.sqs = sb2("sqs", [128, NK, 256])
            bufs.rstd4 = sb2("rstd4", [128, 256], F32)
            bufs.rstd4b = sb2("rstd4b", [128, 256])
            bufs.ktm = sb2("ktm", [128, 2, DH])
            bufs.vtm = sb2("vtm", [128, 2, DH])
            bufs.gneg = sb2("gneg", [128, 2, 2, 128])
            bufs.pprod = sb2("pprod", [128, 2, 2, 2, 128])
            bufs.bsq = sb2("bsq", [128, 2, 2, 128])
            bufs.tpos = sb2("tpos", [128, 2, 128])
            bufs.tneg = sb2("tneg", [128, 2, 128])
            bufs.wtm = sb2("wtm", [128, 2, 2, 128])
            bufs.usb = sb2("usb", [128, H, DH])
            bufs.asb = sb2("asb", [128, 2, 128])
            bufs.otm = sb2("otm", [128, 2, DH])
            bufs.ofm = sb2("ofm", [128, NK, 128])

            bufs.wp_marks = []
            bufs.wp_use = [0]

            @block.gpsimd
            def _(gpsimd):
                s_ldw.inc(gpsimd.dma_start(
                    out=bufs.wo_sb[:],
                    in_=hwo.rearrange("(kc p) n -> p kc n", p=128)), 16)

            @block.vector
            def _(vector):
                s_ldw.wait(vector)
                for kk in range(NK):
                    ins = vector.tensor_scalar(
                        out=bufs.wo_sb[:, kk, :], in0=bufs.wo_sb[:, kk, :],
                        scalar1=grms[:, kk:kk + 1], scalar2=None, op0=ALU.mult)
                s_ve.inc(ins)
            return es2

        def ffn_arena():
            arena_ct[0] += 1
            sfx = "_%d" % arena_ct[0]
            es2 = ExitStack()

            def sb2(name, shape, dt=BF16):
                return es2.enter_context(
                    nc.sbuf_tensor("ar_" + name + sfx, list(shape), dt))
            bufs.w1q = sb2("w1q", [128, 2, NK, 1024])
            bufs.w3q = sb2("w3q", [128, 2, NK, 1024])
            bufs.w2q = sb2("w2q", [128, NK, D])
            bufs.hbuf = sb2("hbuf", [128, NK, ST])
            bufs.hsil = sb2("hsil", [128, 2, ST])
            return es2

        def emit_ln(p):
            for t in range(NT128):
                @block.sync
                def _(sync, t=t):
                    if len(xf_marks) >= 2:
                        sync.wait_ge(s_ve.h, xf_marks[-2])
                    s_ldx.inc(sync.dma_start(
                        out=xres[:, t % 2, :],
                        in_=hx[p * HALF + t * 128: p * HALF + (t + 1) * 128, :]), 16)

                @block.scalar
                def _(scalar, t=t):
                    s_ldx.wait(scalar)
                    s_ve.wait(scalar)
                    ins = scalar.activation(out=sxg[:], in_=xres[:, t % 2, :],
                                            func=AF.Square, accum_out=stat[:, 1:2])
                    s_se.inc(ins)

                @block.vector
                def _(vector, t=t):
                    s_ldx.wait(vector)
                    s_se.wait(vector)
                    vfence(vector, vector.tensor_reduce(stat[:, 0:1],
                                                        xres[:, t % 2, :], AX.X, ALU.add))
                    vfence(vector, vector.tensor_scalar_mul(stat[:, 2:3],
                                                             stat[:, 0:1], -1.0 / D))
                    vfence(vector, vector.tensor_single_scalar(stat[:, 3:4],
                                                               stat[:, 2:3],
                                                               stat[:, 2:3], ALU.mult))
                    ins = vector.tensor_scalar(out=stat[:, 4:5], in0=stat[:, 1:2],
                                               scalar1=cst[:, C_INVD:C_INVD + 1],
                                               scalar2=stat[:, 3:4],
                                               op0=ALU.mult, op1=ALU.subtract)
                    s_ve.inc(ins)

                @block.scalar
                def _(scalar, t=t):
                    s_ve.wait(scalar)
                    ins = scalar.activation(out=stat[:, 5:6], in_=stat[:, 4:5],
                                            func=AF.Sqrt, bias=cst[:, C_EPS:C_EPS + 1])
                    s_se.inc(ins)

                @block.vector
                def _(vector, t=t):
                    s_se.wait(vector)
                    vfence(vector, vector.reciprocal(stat[:, 6:7], stat[:, 5:6]))
                    ins = vector.tensor_scalar(out=xn_tile[:], in0=xres[:, t % 2, :],
                                               scalar1=stat[:, 2:3], scalar2=stat[:, 6:7],
                                               op0=ALU.add, op1=ALU.mult)
                    s_ve.inc(ins)
                    xf_marks.append(s_ve.n)

                @block.tensor
                def _(tensor, t=t):
                    s_ve.wait(tensor)
                    for fc in range(NK):
                        bank = 6 + fc % 2
                        pe_acquire(tensor, bank)
                        ins = tensor.transpose(pt(bank, 128),
                                               xn_tile[:, fc * 128:(fc + 1) * 128],
                                               ident[:])
                        pe_produced(ins, bank)

                        @block.vector
                        def _(vector, t=t, fc=fc, bank=bank):
                            ev_acquire(vector, bank)
                            s_ve.inc(vector.tensor_copy(
                                xn_fm[:, fc, t * 128:(t + 1) * 128], pt(bank, 128)))
                            ev_freed(bank, s_ve)

        def emit_gating():
            for t in range(NT128):
                @block.tensor
                def _(tensor, t=t):
                    if t == 0:
                        s_ve.wait(tensor)
                    bank = t % 2
                    pe_acquire(tensor, bank)
                    for kk in range(NK):
                        ins = tensor.matmul(pf(bank, NE),
                                            xn_fm[:, kk, t * 128:(t + 1) * 128],
                                            gwt[:, kk, :],
                                            start=(kk == 0), stop=(kk == NK - 1))
                    pe_produced(ins, bank)

                @block.vector
                def _(vector, t=t):
                    bank = t % 2
                    ev_acquire(vector, bank)
                    s_ve.inc(vector.tensor_copy(score[:], pf(bank, NE)))
                    ev_freed(bank, s_ve)

                @block.scalar
                def _(scalar, t=t):
                    s_ve.wait(scalar)
                    s_se.inc(scalar.activation(out=gt4[:], in_=score[:], func=AF.Abs))

                @block.vector
                def _(vector, t=t):
                    s_se.wait(vector)
                    vfence(vector, vector.tensor_reduce(stat[:, 0:1], score[:],
                                                        AX.X, ALU.max))
                    vfence(vector, vector.tensor_scalar(out=gt1[:], in0=score[:],
                                                        scalar1=stat[:, 0:1],
                                                        scalar2=None, op0=ALU.is_equal))
                    vfence(vector, vector.tensor_scalar(out=gt2[:], in0=gt4[:],
                                                        scalar1=stat[:, 0:1],
                                                        scalar2=None, op0=ALU.max))
                    vfence(vector, vector.reciprocal(gt2[:], gt2[:]))
                    vfence(vector, vector.tensor_scalar(out=gt3[:], in0=score[:],
                                                        scalar1=cst[:, C_NEG1:C_NEG1 + 1],
                                                        scalar2=stat[:, 0:1],
                                                        op0=ALU.mult, op1=ALU.add))
                    vfence(vector, vector.tensor_mul(gt3[:], gt3[:], gt2[:]))
                    vfence(vector, vector.tensor_single_scalar(gt3[:], gt3[:], JIT2,
                                                               ALU.is_gt))
                    vfence(vector, vector.scalar_tensor_tensor(out=gt3[:], in0=gt3[:],
                                                               scalar=NEG, in1=score[:],
                                                               op0=ALU.mult, op1=ALU.add))
                    ins = vector.tensor_scalar_mul(stat[:, 1:2], stat[:, 0:1], -1.0)
                    s_ve.inc(ins)

                @block.scalar
                def _(scalar, t=t):
                    s_ve.wait(scalar)
                    s_se.inc(scalar.activation(out=gt3[:], in_=gt3[:], func=AF.Exp,
                                               bias=stat[:, 1:2],
                                               accum_out=stat[:, 2:3]))

                @block.vector
                def _(vector, t=t):
                    s_se.wait(vector)
                    vfence(vector, vector.reciprocal(stat[:, 3:4], stat[:, 2:3]))
                    vfence(vector, vector.scalar_tensor_tensor(out=gt2[:], in0=gt1[:],
                                                               scalar=NEG, in1=score[:],
                                                               op0=ALU.mult, op1=ALU.add))
                    vfence(vector, vector.tensor_reduce(stat[:, 4:5], gt2[:],
                                                        AX.X, ALU.max))
                    vfence(vector, vector.tensor_scalar(out=gt4[:], in0=gt4[:],
                                                        scalar1=stat[:, 4:5],
                                                        scalar2=None, op0=ALU.max))
                    vfence(vector, vector.reciprocal(gt4[:], gt4[:]))
                    vfence(vector, vector.tensor_scalar(out=gt3[:], in0=score[:],
                                                        scalar1=cst[:, C_NEG1:C_NEG1 + 1],
                                                        scalar2=stat[:, 4:5],
                                                        op0=ALU.mult, op1=ALU.add))
                    vfence(vector, vector.tensor_mul(gt3[:], gt3[:], gt4[:]))
                    vfence(vector, vector.tensor_single_scalar(gt3[:], gt3[:], JIT2,
                                                               ALU.is_gt))
                    vfence(vector, vector.scalar_tensor_tensor(out=gt3[:], in0=gt3[:],
                                                               scalar=NEG, in1=gt2[:],
                                                               op0=ALU.mult, op1=ALU.add))
                    ins = vector.tensor_scalar_mul(stat[:, 5:6], stat[:, 4:5], -1.0)
                    s_ve.inc(ins)

                @block.scalar
                def _(scalar, t=t):
                    s_ve.wait(scalar)
                    s_se.inc(scalar.activation(out=gt3[:], in_=gt3[:], func=AF.Exp,
                                               bias=stat[:, 5:6],
                                               accum_out=stat[:, 6:7]))

                @block.vector
                def _(vector, t=t):
                    s_se.wait(vector)
                    vfence(vector, vector.reciprocal(stat[:, 7:8], stat[:, 6:7]))
                    vfence(vector, vector.tensor_scalar(out=gt4[:], in0=score[:],
                                                        scalar1=stat[:, 4:5],
                                                        scalar2=None, op0=ALU.is_equal))
                    vfence(vector, vector.tensor_scalar(out=gt1[:], in0=gt1[:],
                                                        scalar1=stat[:, 3:4],
                                                        scalar2=None, op0=ALU.mult))
                    ins = vector.scalar_tensor_tensor(out=wtok[:, t, :], in0=gt4[:],
                                                      scalar=stat[:, 7:8], in1=gt1[:],
                                                      op0=ALU.mult, op1=ALU.add)
                    s_ve.inc(ins)

        def emit_ffn():
            wdone_marks = []
            @block.vector
            def _(vector):
                s_ve.wait(vector)
                ins = vector.memset(g_tm[:], 0.0)
                s_ve.inc(ins)

            for e in range(NE):
                for fq in range(NFQ):
                    idx = e * NFQ + fq
                    par = idx % 2

                    @block.gpsimd
                    def _(gpsimd, e=e, fq=fq, par=par, idx=idx):
                        if idx >= 2:
                            bb, cc = wdone_marks[idx - 2]
                            gpsimd.wait_ge(bank_prod[bb].h, cc)
                        s_ldw.inc(gpsimd.dma_start(
                            out=bufs.w1q[:, par, :, :],
                            in_=hw1[e].rearrange("(kc p) f -> p kc f", p=128)
                            [:, :, fq * 1024:(fq + 1) * 1024]), 16)
                        s_ldw.inc(gpsimd.dma_start(
                            out=bufs.w3q[:, par, :, :],
                            in_=hw3[e].rearrange("(kc p) f -> p kc f", p=128)
                            [:, :, fq * 1024:(fq + 1) * 1024]), 16)

                    need_w = s_ldw.n

                    @block.gpsimd
                    def _(gpsimd, e=e, fq=fq, idx=idx):
                        if idx >= 1:
                            bb, cc = wdone_marks[idx - 1]
                            gpsimd.wait_ge(bank_prod[bb].h, cc)
                        s_ldw.inc(gpsimd.dma_start(
                            out=bufs.w2q[:, :, :],
                            in_=hw2[e, fq * 1024:(fq + 1) * 1024, :]
                            .rearrange("(fc p) h -> p fc h", p=128)), 16)

                    need_w2 = s_ldw.n

                    for tt in range(NST):
                        for fc in range(NK):
                            bA = fc % 2
                            bB = 2 + fc % 2

                            @block.tensor
                            def _(tensor, par=par, tt=tt, fc=fc, bA=bA, bB=bB,
                                  need_w=need_w):
                                if fc == 0 and tt == 0:
                                    tensor.wait_ge(s_ldw.h, need_w)
                                    s_ve.wait(tensor)
                                pe_acquire(tensor, bA)
                                for kk in range(NK):
                                    ins = tensor.matmul(
                                        pf(bA, ST),
                                        bufs.w1q[:, par, kk, fc * 128:(fc + 1) * 128],
                                        xn_fm[:, kk, tt * ST:(tt + 1) * ST],
                                        start=(kk == 0), stop=(kk == NK - 1))
                                pe_produced(ins, bA)
                                pe_acquire(tensor, bB)
                                for kk in range(NK):
                                    ins = tensor.matmul(
                                        pf(bB, ST),
                                        bufs.w3q[:, par, kk, fc * 128:(fc + 1) * 128],
                                        xn_fm[:, kk, tt * ST:(tt + 1) * ST],
                                        start=(kk == 0), stop=(kk == NK - 1))
                                pe_produced(ins, bB)

                            @block.scalar
                            def _(scalar, fc=fc, bA=bA):
                                ev_acquire(scalar, bA)
                                if s_ve.n > 0:
                                    scalar.wait_ge(s_ve.h, s_ve.n - 1)
                                s_se.inc(scalar.activation(
                                    out=bufs.hsil[:, fc % 2, 0:ST],
                                    in_=pf(bA, ST), func=AF.Silu))
                                ev_freed(bA, s_se)

                            @block.vector
                            def _(vector, fc=fc, bB=bB):
                                ev_acquire(vector, bB)
                                s_se.wait(vector)
                                s_ve.inc(vector.tensor_mul(
                                    bufs.hbuf[:, fc, 0:ST], pf(bB, ST),
                                    bufs.hsil[:, fc % 2, 0:ST]))
                                ev_freed(bB, s_ve)

                        hdone = s_ve.n
                        n_w2 = (ST // 128) * 2
                        w2i = 0
                        for t1 in range(ST // 128):
                            for dh in range(2):
                                bG = 4 + (t1 * 2 + dh) % 2
                                w2i += 1
                                is_last = (w2i == n_w2)

                                @block.tensor
                                def _(tensor, par=par, tt=tt, t1=t1, dh=dh, bG=bG,
                                      hdone=hdone, is_last=is_last, need_w2=need_w2):
                                    tensor.wait_ge(s_ve.h, hdone)
                                    if tt == 0 and t1 == 0 and dh == 0:
                                        tensor.wait_ge(s_ldw.h, need_w2)
                                    pe_acquire(tensor, bG)
                                    for fc in range(NK):
                                        ins = tensor.matmul(
                                            pf(bG),
                                            bufs.hbuf[:, fc, t1 * 128:(t1 + 1) * 128],
                                            bufs.w2q[:, fc, dh * 512:(dh + 1) * 512],
                                            start=(fc == 0), stop=(fc == NK - 1))
                                    pe_produced(ins, bG)
                                    if is_last and tt == NST - 1:
                                        wdone_marks.append((bG, bank_prod[bG].n))

                                @block.vector
                                def _(vector, e=e, tt=tt, t1=t1, dh=dh, bG=bG):
                                    ev_acquire(vector, bG)
                                    tile = tt * (ST // 128) + t1
                                    s_ve.inc(vector.scalar_tensor_tensor(
                                        out=g_tm[:, tile, dh * 512:(dh + 1) * 512],
                                        in0=pf(bG),
                                        scalar=wtok[:, tile, e:e + 1],
                                        in1=g_tm[:, tile, dh * 512:(dh + 1) * 512],
                                        op0=ALU.mult, op1=ALU.add))
                                    ev_freed(bG, s_ve)

        def emit_qkv(p, st, mode):
            t0 = st * ST
            plist = [("k", hwk, bufs.kcv, 1, True),
                     ("v", hwv, bufs.vcv, 2, True)]
            if mode == "full":
                plist.insert(0, ("q", hwq, bufs.qcv, 0, True))
            elif mode == "kvqtail":
                plist.insert(0, ("q", hwq, bufs.qcv, 0, False))

            for name, hw_src, cout, tslot, do_conv in plist:
                use = bufs.wp_use[0]
                bufs.wp_use[0] += 1
                wpar = use % 2

                @block.gpsimd
                def _(gpsimd, hw_src=hw_src, use=use, wpar=wpar):
                    if use >= 2:
                        bb, cc = bufs.wp_marks[use - 2]
                        gpsimd.wait_ge(bank_prod[bb].h, cc)
                    s_ldw.inc(gpsimd.dma_start(
                        out=bufs.wproj[:, wpar, :, :],
                        in_=hw_src.rearrange("(kc p) n -> p kc n", p=128)), 16)

                need_wp = s_ldw.n
                for oc in range(NK):
                    bank = oc % 2

                    @block.tensor
                    def _(tensor, wpar=wpar, oc=oc, bank=bank, t0=t0, need_wp=need_wp):
                        if oc == 0:
                            tensor.wait_ge(s_ldw.h, need_wp)
                        pe_acquire(tensor, bank)
                        for kk in range(NK):
                            ins = tensor.matmul(
                                pf(bank, ST),
                                bufs.wproj[:, wpar, kk, oc * 128:(oc + 1) * 128],
                                xn_fm[:, kk, t0:t0 + ST],
                                start=(kk == 0), stop=(kk == NK - 1))
                        pe_produced(ins, bank)
                        if oc == NK - 1:
                            bufs.wp_marks.append((bank, bank_prod[bank].n))

                    @block.vector
                    def _(vector, cout=cout, oc=oc, bank=bank, tslot=tslot,
                          do_conv=do_conv):
                        par = oc % 2
                        ev_acquire(vector, bank)
                        vector.tensor_copy(bufs.linb[:, par, 0:3], tails[:, tslot, oc, :])
                        vfence(vector, vector.tensor_copy(bufs.linb[:, par, 3:3 + ST],
                                                          pf(bank, ST)))
                        ins = vector.tensor_copy(tails[:, tslot, oc, :],
                                                 bufs.linb[:, par, ST:ST + 3])
                        if do_conv:
                            ci = tslot
                            if s_se.n > 0:
                                vector.wait_ge(s_se.h, s_se.n - 1)
                            vfence(vector, vector.tensor_scalar(
                                out=bufs.convacc[:, par, :], in0=bufs.linb[:, par, 0:ST],
                                scalar1=convw[:, oc, ci, 0:1], scalar2=None,
                                op0=ALU.mult))
                            for j in (1, 2, 3):
                                ins = vector.scalar_tensor_tensor(
                                    out=bufs.convacc[:, par, :],
                                    in0=bufs.linb[:, par, j:j + ST],
                                    scalar=convw[:, oc, ci, j:j + 1],
                                    in1=bufs.convacc[:, par, :],
                                    op0=ALU.mult, op1=ALU.add)
                                if j < 3:
                                    vfence(vector, ins)
                        s_ve.inc(ins)
                        ev_freed(bank, s_ve)

                    if do_conv:
                        @block.scalar
                        def _(scalar, cout=cout, oc=oc):
                            s_ve.wait(scalar)
                            ins = scalar.activation(out=cout[:, oc, :],
                                                    in_=bufs.convacc[:, oc % 2, :],
                                                    func=AF.Silu)
                            s_se.inc(ins)

                if do_conv and name in ("q", "k"):
                    for h2 in range(ST // 256):
                        c0 = h2 * 256
                        for oc in range(NK):
                            @block.scalar
                            def _(scalar, cout=cout, oc=oc, c0=c0):
                                if oc == 0:
                                    scalar.wait_ge(bank_prod[2].h, bank_prod[2].n)
                                ins = scalar.activation(
                                    out=bufs.sqs[:, oc, :],
                                    in_=cout[:, oc, c0:c0 + 256], func=AF.Square)
                                if oc == NK - 1:
                                    s_se.inc(ins)

                        @block.tensor
                        def _(tensor):
                            s_se.wait(tensor)
                            pe_acquire(tensor, 2)
                            for oc in range(NK):
                                ins = tensor.matmul(pf(2, 256)[0:H, :], hmm[:, oc, :],
                                                    bufs.sqs[:, oc, :],
                                                    start=(oc == 0), stop=(oc == NK - 1))
                            pe_produced(ins, 2)

                        @block.vector
                        def _(vector):
                            ev_acquire(vector, 2)
                            s_ve.inc(vector.tensor_copy(bufs.rstd4[0:H, :],
                                                        pf(2, 256)[0:H, :]))
                            ev_freed(2, s_ve)

                        @block.scalar
                        def _(scalar):
                            s_ve.wait(scalar)
                            ins = scalar.activation(out=bufs.rstd4[0:H, :],
                                                    in_=bufs.rstd4[0:H, :],
                                                    func=AF.Sqrt,
                                                    bias=cst[0:H, C_L2EPS:C_L2EPS + 1])
                            s_se.inc(ins)

                        @block.vector
                        def _(vector):
                            s_se.wait(vector)
                            vfence(vector, vector.reciprocal(bufs.rstd4[0:H, :],
                                                             bufs.rstd4[0:H, :]))
                            ins = vector.tensor_copy(bufs.rstd4b[0:H, :],
                                                     bufs.rstd4[0:H, :])
                            s_ve.inc(ins)

                        for hh in range(H):
                            @block.tensor
                            def _(tensor, hh=hh):
                                s_ve.wait(tensor)
                                pe_acquire(tensor, 3)
                                ins = tensor.matmul(pf(3, 256), hsel[0:4, hh, :],
                                                    bufs.rstd4b[0:4, :],
                                                    start=True, stop=True)
                                pe_produced(ins, 3)

                            @block.vector
                            def _(vector, cout=cout, hh=hh, c0=c0):
                                ev_acquire(vector, 3)
                                for sub in range(2):
                                    fc = 2 * hh + sub
                                    ins = vector.tensor_mul(
                                        cout[:, fc, c0:c0 + 256],
                                        cout[:, fc, c0:c0 + 256], pf(3, 256))
                                s_ve.inc(ins)
                                ev_freed(3, s_ve)

            for i in range(CPS):
                ch = st * CPS + i

                @block.tensor
                def _(tensor, i=i, t0=t0):
                    pe_acquire(tensor, 2)
                    for kk in range(NK):
                        ins = tensor.matmul(
                            pf(2, H), xn_fm[:, kk, t0 + i * 128:t0 + (i + 1) * 128],
                            wb_sb[:, kk, :], start=(kk == 0), stop=(kk == NK - 1))
                    pe_produced(ins, 2)

                @block.scalar
                def _(scalar, ch=ch):
                    ev_acquire(scalar, 2)
                    s_se.inc(scalar.activation(out=beta[:, ch, :], in_=pf(2, H),
                                               func=AF.Sigmoid))
                    ev_freed(2, s_se)

                @block.vector
                def _(vector, ch=ch):
                    s_se.wait(vector)
                    ins = vector.tensor_scalar_mul(nbeta[:, ch, :], beta[:, ch, :], -1.0)
                    s_ve.inc(ins)

        def emit_delta_chunk(p, st, i, own):
            ch = st * CPS + i
            cs = i * CC
            t128 = st * (ST // 128) + i

            for hh in range(H):
                hp = hh % 2
                bGA = hp
                bS1 = 2 + hp
                bS2 = 4 + hp
                bT = 6 + hp

                @block.tensor
                def _(tensor, hh=hh, hp=hp, bT=bT, cs=cs):
                    for src, dst in ((bufs.kcv, bufs.ktm), (bufs.vcv, bufs.vtm)):
                        for sub in range(2):
                            pe_acquire(tensor, bT)
                            ins = tensor.transpose(
                                pt(bT, 128), src[:, 2 * hh + sub, cs:cs + CC], ident[:])
                            pe_produced(ins, bT)

                            @block.vector
                            def _(vector, dst=dst, hp=hp, sub=sub, bT=bT):
                                ev_acquire(vector, bT)
                                s_ve.inc(vector.tensor_copy(
                                    dst[:, hp, sub * 128:(sub + 1) * 128], pt(bT, 128)))
                                ev_freed(bT, s_ve)

                @block.tensor
                def _(tensor, hh=hh, bGA=bGA, cs=cs):
                    s_ve.wait(tensor)
                    pe_acquire(tensor, bGA)
                    for sub in range(2):
                        ins = tensor.matmul(pf(bGA, 128),
                                            bufs.kcv[:, 2 * hh + sub, cs:cs + CC],
                                            bufs.kcv[:, 2 * hh + sub, cs:cs + CC],
                                            start=(sub == 0), stop=(sub == 1))
                    pe_produced(ins, bGA)

                @block.vector
                def _(vector, hh=hh, hp=hp, bGA=bGA, ch=ch):
                    ev_acquire(vector, bGA)
                    s_ve.inc(vector.scalar_tensor_tensor(
                        out=bufs.gneg[:, hp, 0, :], in0=pf(bGA, 128),
                        scalar=nbeta[:, ch, hh:hh + 1], in1=msu[:],
                        op0=ALU.mult, op1=ALU.mult))
                    ev_freed(bGA, s_ve)

                @block.tensor
                def _(tensor, hp=hp, bT=bT):
                    s_ve.wait(tensor)
                    pe_acquire(tensor, bT)
                    ins = tensor.transpose(pt(bT, 128), bufs.gneg[:, hp, 0, :], ident[:])
                    pe_produced(ins, bT)

                @block.vector
                def _(vector, hp=hp, bT=bT):
                    ev_acquire(vector, bT)
                    vector.tensor_copy(bufs.gneg[:, hp, 1, :], pt(bT, 128))
                    vector.tensor_add(bufs.pprod[:, hp, 0, 0, :], ident[:],
                                      bufs.gneg[:, hp, 0, :])
                    ins = vector.tensor_add(bufs.pprod[:, hp, 0, 1, :], ident[:],
                                            bufs.gneg[:, hp, 1, :])
                    s_ve.inc(ins)
                    ev_freed(bT, s_ve)

                for j in range(1, 7):
                    last = (j == 6)

                    @block.tensor
                    def _(tensor, hp=hp, bS1=bS1, j=j):
                        s_ve.wait(tensor)
                        if j == 1:
                            srcn, srct = bufs.gneg[:, hp, 0, :], bufs.gneg[:, hp, 1, :]
                        else:
                            srcn, srct = bufs.bsq[:, hp, 0, :], bufs.bsq[:, hp, 1, :]
                        pe_acquire(tensor, bS1)
                        ins = tensor.matmul(pf(bS1, 128, 0), srct, srcn,
                                            start=True, stop=True)
                        ins = tensor.matmul(pf(bS1, 128, 128), srcn, srct,
                                            start=True, stop=True)
                        pe_produced(ins, bS1)

                    @block.vector
                    def _(vector, hp=hp, bS1=bS1, j=j):
                        ev_acquire(vector, bS1)
                        vector.tensor_copy(bufs.bsq[:, hp, 0, :], pf(bS1, 128, 0))
                        s_ve.inc(vector.tensor_copy(bufs.bsq[:, hp, 1, :],
                                                    pf(bS1, 128, 128)))
                        ev_freed(bS1, s_ve)

                    @block.tensor
                    def _(tensor, hp=hp, bS2=bS2, j=j, last=last):
                        s_ve.wait(tensor)
                        rot = (j - 1) % 2
                        pn = bufs.pprod[:, hp, rot, 0, :]
                        ptr = bufs.pprod[:, hp, rot, 1, :]
                        pe_acquire(tensor, bS2)
                        ins = tensor.matmul(pf(bS2, 128, 0), ident[:], pn,
                                            start=True, stop=False)
                        ins = tensor.matmul(pf(bS2, 128, 0), ptr, bufs.bsq[:, hp, 0, :],
                                            start=False, stop=True)
                        if not last:
                            ins = tensor.matmul(pf(bS2, 128, 128), ident[:], ptr,
                                                start=True, stop=False)
                            ins = tensor.matmul(pf(bS2, 128, 128), bufs.bsq[:, hp, 0, :],
                                                ptr, start=False, stop=True)
                        pe_produced(ins, bS2)

                    @block.vector
                    def _(vector, hp=hp, bS2=bS2, j=j, last=last):
                        ev_acquire(vector, bS2)
                        rot = j % 2
                        if last:
                            vector.tensor_copy(bufs.tpos[:, hp, :], pf(bS2, 128, 0))
                            ins = vector.tensor_scalar_mul(
                                bufs.tneg[:, hp, :], pf(bS2, 128, 0), -1.0)
                        else:
                            vector.tensor_copy(bufs.pprod[:, hp, rot, 0, :], pf(bS2, 128, 0))
                            ins = vector.tensor_copy(bufs.pprod[:, hp, rot, 1, :],
                                                     pf(bS2, 128, 128))
                        s_ve.inc(ins)
                        ev_freed(bS2, s_ve)

                @block.tensor
                def _(tensor, hp=hp, bS1=bS1):
                    s_ve.wait(tensor)
                    pe_acquire(tensor, bS1)
                    ins = tensor.matmul(pf(bS1, 128, 0), bufs.ktm[:, hp, 0:128],
                                        bufs.tneg[:, hp, :], start=True, stop=True)
                    ins = tensor.matmul(pf(bS1, 128, 128), bufs.ktm[:, hp, 128:256],
                                        bufs.tneg[:, hp, :], start=True, stop=True)
                    pe_produced(ins, bS1)

                @block.vector
                def _(vector, hp=hp, bS1=bS1):
                    ev_acquire(vector, bS1)
                    vector.tensor_copy(bufs.wtm[:, hp, 0, :], pf(bS1, 128, 0))
                    s_ve.inc(vector.tensor_copy(bufs.wtm[:, hp, 1, :],
                                                pf(bS1, 128, 128)))
                    ev_freed(bS1, s_ve)

                @block.tensor
                def _(tensor, hh=hh, hp=hp, bS2=bS2):
                    s_ve.wait(tensor)
                    pe_acquire(tensor, bS2)
                    ins = tensor.matmul(pf(bS2, 256), bufs.tpos[:, hp, :], bufs.vtm[:, hp, :],
                                        start=True, stop=False)
                    ins = tensor.matmul(pf(bS2, 256), bufs.wtm[:, hp, 0, :],
                                        Sbf[:, hh, 0, :], start=False, stop=False)
                    ins = tensor.matmul(pf(bS2, 256), bufs.wtm[:, hp, 1, :],
                                        Sbf[:, hh, 1, :], start=False, stop=True)
                    pe_produced(ins, bS2)

                @block.vector
                def _(vector, hh=hh, bS2=bS2, ch=ch):
                    ev_acquire(vector, bS2)
                    s_ve.inc(vector.tensor_scalar(
                        out=bufs.usb[:, hh, :], in0=pf(bS2, 256),
                        scalar1=beta[:, ch, hh:hh + 1], scalar2=None, op0=ALU.mult))
                    ev_freed(bS2, s_ve)

                if own:
                    @block.tensor
                    def _(tensor, hh=hh, bGA=bGA, cs=cs):
                        pe_acquire(tensor, bGA)
                        for sub in range(2):
                            ins = tensor.matmul(pf(bGA, 128),
                                                bufs.kcv[:, 2 * hh + sub, cs:cs + CC],
                                                bufs.qcv[:, 2 * hh + sub, cs:cs + CC],
                                                start=(sub == 0), stop=(sub == 1))
                        pe_produced(ins, bGA)

                    @block.vector
                    def _(vector, hp=hp, bGA=bGA):
                        ev_acquire(vector, bGA)
                        s_ve.inc(vector.tensor_mul(bufs.asb[:, hp, :], pf(bGA, 128),
                                                   mui[:]))
                        ev_freed(bGA, s_ve)

                    @block.tensor
                    def _(tensor, hh=hh, hp=hp, bS1=bS1, cs=cs):
                        s_ve.wait(tensor)
                        pe_acquire(tensor, bS1)
                        ins = tensor.matmul(pf(bS1, 256), bufs.qcv[:, 2 * hh, cs:cs + CC],
                                            Sbf[:, hh, 0, :], start=True, stop=False)
                        ins = tensor.matmul(pf(bS1, 256), bufs.qcv[:, 2 * hh + 1, cs:cs + CC],
                                            Sbf[:, hh, 1, :], start=False, stop=False)
                        ins = tensor.matmul(pf(bS1, 256), bufs.asb[:, hp, :], bufs.usb[:, hh, :],
                                            start=False, stop=True)
                        pe_produced(ins, bS1)

                    @block.scalar
                    def _(scalar, hp=hp, bS1=bS1):
                        ev_acquire(scalar, bS1)
                        sfence(scalar, scalar.activation(out=sxg[:, 0:256],
                                                         in_=pf(bS1, 256),
                                                         func=AF.Square,
                                                         accum_out=stat[:, 8:9]))
                        s_ve.wait(scalar)
                        ins = scalar.activation(out=stat[:, 9:10], in_=stat[:, 8:9],
                                                func=AF.Sqrt,
                                                scale=cst[:, C_INV256:C_INV256 + 1],
                                                bias=cst[:, C_EPS:C_EPS + 1])
                        s_se.inc(ins)

                    @block.vector
                    def _(vector, hp=hp, bS1=bS1):
                        s_se.wait(vector)
                        vfence(vector, vector.reciprocal(stat[:, 10:11], stat[:, 9:10]))
                        s_ve.inc(vector.tensor_scalar(
                            out=bufs.otm[:, hp, :], in0=pf(bS1, 256),
                            scalar1=stat[:, 10:11], scalar2=None, op0=ALU.mult))
                        ev_freed(bS1, s_ve)

                    @block.tensor
                    def _(tensor, hh=hh, hp=hp, bT=bT):
                        s_ve.wait(tensor)
                        for sub in range(2):
                            pe_acquire(tensor, bT)
                            ins = tensor.transpose(pt(bT, 128),
                                                   bufs.otm[:, hp, sub * 128:(sub + 1) * 128],
                                                   ident[:])
                            pe_produced(ins, bT)

                            @block.vector
                            def _(vector, hh=hh, sub=sub, bT=bT):
                                ev_acquire(vector, bT)
                                s_ve.inc(vector.tensor_copy(
                                    bufs.ofm[:, 2 * hh + sub, :], pt(bT, 128)))
                                ev_freed(bT, s_ve)

                @block.tensor
                def _(tensor, hh=hh, hp=hp, bGA=bGA):
                    s_ve.wait(tensor)
                    pe_acquire(tensor, bGA)
                    ins = tensor.matmul(pf(bGA, 256, 0), bufs.ktm[:, hp, 0:128],
                                        bufs.usb[:, hh, :], start=True, stop=True)
                    ins = tensor.matmul(pf(bGA, 256, 256), bufs.ktm[:, hp, 128:256],
                                        bufs.usb[:, hh, :], start=True, stop=True)
                    pe_produced(ins, bGA)

                @block.vector
                def _(vector, hh=hh, bGA=bGA):
                    ev_acquire(vector, bGA)
                    vector.tensor_add(S32[:, hh, 0, :], S32[:, hh, 0, :],
                                      pf(bGA, 256, 0))
                    vfence(vector, vector.tensor_add(S32[:, hh, 1, :], S32[:, hh, 1, :],
                                                     pf(bGA, 256, 256)))
                    s_ve.inc(vector.tensor_copy(Sbf[:, hh, :, :], S32[:, hh, :, :]))
                    ev_freed(bGA, s_ve)

            if not own:
                return

            @block.tensor
            def _(tensor, t128=t128):
                s_ve.wait(tensor)
                for fc in range(NK):
                    bank = 6 + fc % 2
                    pe_acquire(tensor, bank)
                    ins = tensor.transpose(
                        pt(bank, 128), xn_fm[:, fc, t128 * 128:(t128 + 1) * 128],
                        ident[:])
                    pe_produced(ins, bank)

                    @block.vector
                    def _(vector, fc=fc, bank=bank):
                        ev_acquire(vector, bank)
                        s_ve.inc(vector.tensor_copy(
                            xn_tile[:, fc * 128:(fc + 1) * 128], pt(bank, 128)))
                        ev_freed(bank, s_ve)

            for dh in range(2):
                bY = dh

                @block.tensor
                def _(tensor, dh=dh, bY=bY):
                    s_ve.wait(tensor)
                    pe_acquire(tensor, bY)
                    for kk in range(NK):
                        ins = tensor.matmul(pf(bY), bufs.ofm[:, kk, :],
                                            bufs.wo_sb[:, kk, dh * 512:(dh + 1) * 512],
                                            start=(kk == 0), stop=(kk == NK - 1))
                    pe_produced(ins, bY)

                @block.vector
                def _(vector, dh=dh, bY=bY, t128=t128):
                    ev_acquire(vector, bY)
                    vector.wait_ge(s_out.h, s_out.n)
                    s_ve.inc(vector.tensor_add(
                        ysb[:, dh * 512:(dh + 1) * 512], pf(bY),
                        xn_tile[:, dh * 512:(dh + 1) * 512]))
                    ev_freed(bY, s_ve)

            @block.scalar
            def _(scalar):
                s_ve.wait(scalar)
                ins = scalar.activation(out=sxg[:], in_=ysb[:], func=AF.Square,
                                        accum_out=stat[:, 1:2])
                s_se.inc(ins)

            @block.vector
            def _(vector):
                s_se.wait(vector)
                vfence(vector, vector.tensor_reduce(stat[:, 0:1], ysb[:],
                                                    AX.X, ALU.add))
                vfence(vector, vector.tensor_scalar_mul(stat[:, 2:3],
                                                         stat[:, 0:1], -1.0 / D))
                vfence(vector, vector.tensor_single_scalar(stat[:, 3:4], stat[:, 2:3],
                                                           stat[:, 2:3], ALU.mult))
                ins = vector.tensor_scalar(out=stat[:, 4:5], in0=stat[:, 1:2],
                                           scalar1=cst[:, C_INVD:C_INVD + 1],
                                           scalar2=stat[:, 3:4],
                                           op0=ALU.mult, op1=ALU.subtract)
                s_ve.inc(ins)

            @block.scalar
            def _(scalar):
                s_ve.wait(scalar)
                ins = scalar.activation(out=stat[:, 5:6], in_=stat[:, 4:5],
                                        func=AF.Sqrt, bias=cst[:, C_EPS:C_EPS + 1])
                s_se.inc(ins)

            @block.vector
            def _(vector, t128=t128):
                s_se.wait(vector)
                vfence(vector, vector.reciprocal(stat[:, 6:7], stat[:, 5:6]))
                vector.tensor_scalar(out=ssb[:], in0=ysb[:], scalar1=stat[:, 2:3],
                                     scalar2=stat[:, 6:7], op0=ALU.add, op1=ALU.mult)
                vector.tensor_mul(ssb[:], ssb[:], lngb[:, 0, :])
                ins = vector.tensor_add(ssb[:], ssb[:], lngb[:, 1, :])
                s_ve.inc(ins)

            @block.scalar
            def _(scalar, t128=t128):
                s_ve.wait(scalar)
                ins = scalar.activation(out=gsil[:], in_=g_tm[:, t128, :], func=AF.Silu)
                s_se.inc(ins)

            @block.sync
            def _(sync, p=p, t128=t128):
                if len(xf_marks) >= 2:
                    sync.wait_ge(s_ve.h, xf_marks[-2])
                s_ldx.inc(sync.dma_start(
                    out=xres[:, t128 % 2, :],
                    in_=hx[p * HALF + t128 * 128: p * HALF + (t128 + 1) * 128, :]), 16)

            @block.vector
            def _(vector, p=p, t128=t128):
                s_se.wait(vector)
                s_ldx.wait(vector)
                vector.tensor_mul(sxg[:], ssb[:], gsil[:])
                ins = vector.tensor_add(ysb[:], xres[:, t128 % 2, :], sxg[:])
                s_ve.inc(ins)
                xf_marks.append(s_ve.n)

            @block.sync
            def _(sync, p=p, t128=t128):
                s_ve.wait(sync)
                row0 = (p - N_HALO) * HALF + t128 * 128
                s_out.inc(sync.dma_start(out=hout[row0:row0 + 128, :],
                                         in_=ysb[:]), 16)

            tag = "%d_%d_%d" % (p - N_HALO, st, i)
            dbg("ofm_" + tag, bufs.ofm[:])
            dbg("ssb_" + tag, ssb[:])
            dbg("usb_" + tag, bufs.usb[:])

        for p in range(N_HALO):
            with attn_arena():
                emit_ln(p)
                barrier()
                for st in range(NST):
                    mode = "kvqtail" if (p == N_HALO - 1 and st == NST - 1) else "kv"
                    emit_qkv(p, st, mode)
                    barrier()
                    for i in range(CPS):
                        emit_delta_chunk(p, st, i, own=False)
                    barrier()
            barrier()

        for pp in range(N_OWNH):
            p = N_HALO + pp
            emit_ln(p)
            barrier()
            dbg("xn_fm_%d" % pp, xn_fm[:])
            emit_gating()
            barrier()
            dbg("wtok_%d" % pp, wtok[:])
            with ffn_arena():
                emit_ffn()
                barrier()
            barrier()
            dbg("g_tm_%d" % pp, g_tm[:])
            with attn_arena():
                for st in range(NST):
                    emit_qkv(p, st, "full")
                    barrier()
                    dbg("qcv_%d_%d" % (pp, st), bufs.qcv[:])
                    dbg("kcv_%d_%d" % (pp, st), bufs.kcv[:])
                    dbg("vcv_%d_%d" % (pp, st), bufs.vcv[:])
                    dbg("beta_%d_%d" % (pp, st), beta[:])
                    for i in range(CPS):
                        emit_delta_chunk(p, st, i, own=True)
                    barrier()
                    dbg("Sbf_%d_%d" % (pp, st), Sbf[:])
            barrier()

        @block.sync
        def _(sync):
            sync.wait_ge(s_out.h, s_out.n)

    out_aps = {"out": hout}
    for n in dbg_outs:
        out_aps["dbg_" + n] = dbg_outs[n]
    return nc, out_aps


def _prep_consts(cfg):
    ST = cfg["ST"]
    ident = np.eye(128, dtype=np.float32).astype(BFNP)
    msu = np.triu(np.ones((128, 128), np.float32), 1)
    mui = np.triu(np.ones((128, 128), np.float32), 0)
    onesr = np.ones((1, 128), np.float32).astype(BFNP)
    hm = np.zeros((128, NK, H), np.float32)
    for c in range(NK):
        hm[:, c, c // 2] = 1.0
    hm = hm.astype(BFNP)
    hselv = np.zeros((128, H, 128), np.float32)
    for hh in range(H):
        hselv[hh, hh, :] = 1.0
    hselv = hselv.astype(BFNP)
    cstv = np.zeros((128, 8), np.float32)
    cstv[:, C_EPS] = 1e-5
    cstv[:, C_L2EPS] = 1e-6
    cstv[:, C_INV256] = 1.0 / DH
    cstv[:, C_NEG1] = -1.0
    cstv[:, C_INVD] = 1.0 / D
    return ident, msu, mui, onesr, hm, cstv, hselv


def kernel(L0_attn_x, L0_attn_Wq, L0_attn_Wk, L0_attn_Wv, L0_attn_Wb,
           L0_attn_conv_q, L0_attn_conv_k, L0_attn_conv_v, L0_attn_g_rms,
           L0_attn_Wo, L0_attn_ln_g, L0_attn_ln_b, L0_ffn_gate_w, L0_ffn_w1,
           L0_ffn_w2, L0_ffn_w3):
    cfg = full_cfg()
    T_OWN = cfg["T_own"]
    x = np.ascontiguousarray(L0_attn_x, dtype=np.float32)
    B, T, _ = x.shape

    nc, _out_aps = build_program(cfg)

    w1t = np.ascontiguousarray(np.transpose(L0_ffn_w1, (0, 2, 1)))
    w3t = np.ascontiguousarray(np.transpose(L0_ffn_w3, (0, 2, 1)))
    w2t = np.ascontiguousarray(np.transpose(L0_ffn_w2, (0, 2, 1)))
    gate_wt = np.ascontiguousarray(L0_ffn_gate_w.T)

    def conv_r(w):
        return np.ascontiguousarray(w.T.reshape(NK, 128, 4).transpose(1, 0, 2))
    convw = np.ascontiguousarray(
        np.stack([conv_r(L0_attn_conv_q), conv_r(L0_attn_conv_k),
                  conv_r(L0_attn_conv_v)], axis=2))
    g_rms_r = np.ascontiguousarray(
        np.stack([L0_attn_g_rms[(c % 2) * 128:(c % 2) * 128 + 128]
                  for c in range(NK)], axis=1))
    ident, msu, mui, onesr, hm, cstv, hselv = _prep_consts(cfg)

    shared = dict(
        headsel=hselv,
        wq=np.ascontiguousarray(L0_attn_Wq), wk=np.ascontiguousarray(L0_attn_Wk),
        wv=np.ascontiguousarray(L0_attn_Wv), wo=np.ascontiguousarray(L0_attn_Wo),
        wb=np.ascontiguousarray(L0_attn_Wb), convw=convw, g_rms_r=g_rms_r,
        ln_g=np.ascontiguousarray(L0_attn_ln_g[None, :]),
        ln_b=np.ascontiguousarray(L0_attn_ln_b[None, :]),
        gate_wt=gate_wt, w1t=w1t, w3t=w3t, w2t=w2t, cst=cstv, ident=ident,
        mask_su=msu, mask_ui=mui, onesrow=onesr, headmask=hm)

    in_maps = []
    zeros_half = np.zeros((T_OWN, D), np.float32)
    for c in range(8):
        b, oh = c // 2, c % 2
        halo = x[b, 0:T_OWN] if oh == 1 else zeros_half
        own = x[b, oh * T_OWN:(oh + 1) * T_OWN]
        x_att = np.ascontiguousarray(np.concatenate([halo, own], axis=0))
        m = dict(shared)
        m["x_att"] = x_att
        in_maps.append(m)

    res = run_bass_kernel_spmd(nc, in_maps, core_ids=list(range(8)))
    out = np.empty((B, T, D), np.float32)
    for c in range(8):
        b, oh = c // 2, c % 2
        out[b, oh * T_OWN:(oh + 1) * T_OWN] = res.results[c]["out"]
    return out
